# version 01
# baseline (speedup 1.0000x reference)
import numpy as np
import ml_dtypes
import concourse.bass as bass
import concourse.mybir as mybir
from concourse.bass_types import AP
from concourse.bass_utils import run_bass_kernel_spmd

F32 = mybir.dt.float32
BF16 = mybir.dt.bfloat16
I32 = mybir.dt.int32
AF = mybir.ActivationFunctionType
ALU = mybir.AluOpType
AX = mybir.AxisListType
BF = ml_dtypes.bfloat16

NCORES = 8
T = 2048
NT = 16
D = 1024


_STEP = {"sync": 16, "gpsimd": 16, "scalar": 1, "vector": 1, "tensor": 1}
_ENGS = ["sync", "gpsimd", "scalar", "vector", "tensor"]


class Prog:

    NPOOL = 12

    def __init__(self):
        self.nodes = []
        self.local_idx = []
        self._eng_count = {e: 0 for e in _STEP}

    def add(self, eng, emit, deps=()):
        assert eng in _STEP, eng
        nid = len(self.nodes)
        deps = [d for d in deps if d is not None]
        for d in deps:
            assert 0 <= d < nid, (d, nid)
        self.nodes.append((eng, emit, deps))
        self.local_idx.append(self._eng_count[eng])
        self._eng_count[eng] += 1
        return nid

    def sem_names(self):
        names = ["eng_" + e for e in ("scalar", "vector", "tensor")]
        for q in ("sync", "gpsimd"):
            for s in range(self.NPOOL):
                names.append(f"dma_{q}_{s}")
        return names

    def realize(self, nc, block, sems, final_deps):
        P = self.NPOOL
        target = {}
        for nid, (eng, _, _) in enumerate(self.nodes):
            li = self.local_idx[nid]
            if eng in ("sync", "gpsimd"):
                target[nid] = (f"dma_{eng}_{li % P}", 16 * (li // P + 1))
            else:
                target[nid] = ("eng_" + eng, li + 1)

        fwaits = {}
        for d in final_deps:
            sn, v = target[d]
            fwaits[sn] = max(fwaits.get(sn, 0), v)

        def make_stream(eng_name):
            stream = [nid for nid, (e, _, _) in enumerate(self.nodes) if e == eng_name]
            is_dma = eng_name in ("sync", "gpsimd")

            def body(e):
                waited = {}
                for nid in stream:
                    _, emit, deps = self.nodes[nid]
                    for d in deps:
                        den = self.nodes[d][0]
                        if (not is_dma) and den == eng_name:
                            continue
                        sn, dval = target[d]
                        if dval > waited.get(sn, 0):
                            e.wait_ge(sems[sn], dval)
                            waited[sn] = dval
                    sn, val = target[nid]
                    if is_dma and val > 16:
                        pv = val - 16
                        if pv > waited.get(sn, 0):
                            e.wait_ge(sems[sn], pv)
                            waited[sn] = pv
                    last = emit(e)
                    if last is None:
                        raise RuntimeError(f"node {nid} on {eng_name} emitted nothing")
                    last.then_inc(sems[sn], 16 if is_dma else 1)
                if eng_name == "vector":
                    for sn2, v in fwaits.items():
                        if v > waited.get(sn2, 0):
                            e.wait_ge(sems[sn2], v)
            return body

        for eng_name in _ENGS:
            if self._eng_count[eng_name] == 0 and eng_name != "vector":
                continue
            getattr(block, eng_name)(make_stream(eng_name))

    def dma_barrier(self, dma_nodes):
        return list(dma_nodes[-self.NPOOL:])


class Ctx:
    def __init__(self, name="prog"):
        self.nc = bass.Bass("TRN2", target_bir_lowering=False, debug=False)
        self.prog = Prog()
        self._stack = None

    def __enter__(self):
        from contextlib import ExitStack
        self._stack = ExitStack()
        self._stack.__enter__()
        return self

    def __exit__(self, *a):
        if self._stack is not None:
            self._stack.__exit__(*a)
            self._stack = None
        return False

    def sb(self, name, shape, dt=F32):
        return self._stack.enter_context(self.nc.sbuf_tensor(name, shape, dt))

    def ps(self, name, shape, dt=F32):
        return self._stack.enter_context(self.nc.psum_tensor(name, shape, dt))

    def din(self, name, shape, dt=F32):
        return self.nc.dram_tensor(name, shape, dt, kind="ExternalInput").ap()

    def dout(self, name, shape, dt=F32):
        return self.nc.dram_tensor(name, shape, dt, kind="ExternalOutput").ap()

    def dtmp(self, name, shape, dt=F32):
        return self.nc.dram_tensor(name, shape, dt).ap()

    def finish(self, store_deps):
        nc = self.nc
        sems = {n: self._stack.enter_context(nc.semaphore("sem_" + n))
                for n in self.prog.sem_names()}
        block = self._stack.enter_context(nc.Block())
        self.prog.realize(nc, block, sems, store_deps)
        self._stack.__exit__(None, None, None)
        self._stack = None
        return nc



def emit_ln_tile(ctx, x_tile_src, xn_bf, scratch, ln_deps, width=D):
    p = ctx.prog
    inv = 1.0 / width
    n1 = p.add("vector", lambda e: e.reduce_sum(scratch["sum"][:], x_tile_src, axis=AX.X), ln_deps)
    n2 = p.add("scalar", lambda e: e.activation(scratch["sq"][:], x_tile_src, AF.Square,
                                                accum_out=scratch["msq"][:]), ln_deps + [n1])

    def mu_var0(e):
        e.tensor_scalar(scratch["mu"][:], scratch["sum"][:], inv, None, ALU.mult)
        return e.tensor_scalar(scratch["var"][:], scratch["msq"][:], inv, None, ALU.mult)
    n3 = p.add("vector", mu_var0, [n1, n2])
    n4 = p.add("scalar", lambda e: e.activation(scratch["m2"][:], scratch["mu"][:], AF.Square), [n3])
    n5 = p.add("vector", lambda e: e.tensor_tensor(scratch["var"][:], scratch["var"][:],
                                                   scratch["m2"][:], ALU.subtract), [n4])
    n6 = p.add("scalar", lambda e: e.activation(scratch["rstd"][:], scratch["var"][:], AF.Sqrt,
                                                bias=scratch["eps"][:, 0:1]), [n5])
    n7 = p.add("vector", lambda e: e.reciprocal(scratch["rstd"][:], scratch["rstd"][:]), [n6])
    n8 = p.add("scalar", lambda e: e.activation(scratch["nb"][:], scratch["mu"][:], AF.Copy,
                                                scale=scratch["rstd"][:, 0:1]), [n7])
    n9 = p.add("vector", lambda e: e.tensor_scalar(xn_bf[:], x_tile_src,
                                                   scratch["rstd"][:, 0:1],
                                                   scratch["nb"][:, 0:1],
                                                   ALU.mult, ALU.subtract), [n8])
    return n9


def emit_transposes(ctx, xn_bf, xnT_list, ident, tp_psum, dep, nchunks=8, col0=0,
                    prev_round=None):
    p = ctx.prog
    evs = []
    for cg in range(0, nchunks, 4):
        n_in_grp = min(4, nchunks - cg)

        def tmm(e, cg=cg, n=n_in_grp):
            last = None
            for j in range(n):
                ci = cg + j
                last = e.transpose(tp_psum[:, j * 128:(j + 1) * 128],
                                   xn_bf[:, ci * 128:(ci + 1) * 128], ident[:])
            return last
        tdeps = [dep] + ([evs[-1]] if evs else ([prev_round] if prev_round else []))
        tn = p.add("tensor", tmm, tdeps)

        def tev(e, cg=cg, n=n_in_grp):
            last = None
            for j in range(n):
                ci = cg + j
                last = e.tensor_copy(xnT_list[ci][:, col0:col0 + 128],
                                     tp_psum[:, j * 128:(j + 1) * 128])
            return last
        evs.append(p.add("vector", tev, [tn]))
    return evs


def np_bf16(x):
    return np.ascontiguousarray(x).astype(BF)



TPAD = 2304
NW = 17
NCALLS = 258


def build_l0a(debug=False, dbg_ext=None, dbg_qdump=None):
    with Ctx("l0a") as ctx:
        nc, p = ctx.nc, ctx.prog
        xin = ctx.din("xin", [2176, D])
        wqT = ctx.din("wqT", [D, D])
        wkT = ctx.din("wkT", [D, D])
        wvT = ctx.din("wvT", [D, D])
        bq = ctx.din("bq", [1, D])
        bv = ctx.din("bv", [1, D])
        bk = ctx.din("bk", [128, 8])
        smatA = ctx.din("smatA", [16, 128, 128], BF16)
        smatB = ctx.din("smatB", [16, 128, 128], BF16)
        maskd = ctx.din("mask", [128, 128])
        identd = ctx.din("ident", [128, 128], BF16)
        x1out = ctx.dout("x1", [T, D])

        if dbg_ext is None:
            dbg_ext = bool(debug)
        if dbg_qdump is None:
            dbg_qdump = bool(debug)
        dmk = ctx.dout if dbg_ext else ctx.dtmp
        kg2 = dmk("kg2", [64, 16 * TPAD], BF16)
        vdram = dmk("vdram", [TPAD, D], BF16)
        attnd = dmk("attnd", [TPAD, D], BF16)
        qdump = ctx.dout("qdump", [T, D], BF16) if dbg_qdump else None

        wq_sb = [ctx.sb(f"wq{i}", [128, D], BF16) for i in range(8)]
        wk_sb = [ctx.sb(f"wk{i}", [128, D], BF16) for i in range(8)]
        wv_sb = [ctx.sb(f"wv{i}", [128, D], BF16) for i in range(8)]
        sA = [ctx.sb(f"sA{i}", [128, 128], BF16) for i in range(16)]
        sB = [ctx.sb(f"sB{i}", [128, 128], BF16) for i in range(16)]
        mask = ctx.sb("mask_sb2", [128, 128])
        ident = ctx.sb("ident_sb", [128, 128], BF16)
        ones1 = ctx.sb("ones1", [1, 128], BF16)
        onescol = ctx.sb("onescol", [128, 1], BF16)
        bq_bc = ctx.sb("bq_bc", [128, D])
        bv_bc = ctx.sb("bv_bc", [128, D])
        bk_sb = ctx.sb("bk_sb", [128, 8])
        bqr = ctx.sb("bqr", [1, D], BF16)
        bvr = ctx.sb("bvr", [1, D], BF16)

        xtile = [ctx.sb(f"xt{i}", [128, D]) for i in range(2)]
        xn_bf = [ctx.sb(f"xnbf{i}", [128, D], BF16) for i in range(2)]
        xnT = [ctx.sb(f"xnT{i}", [128, 128], BF16) for i in range(8)]
        q_tok = [ctx.sb(f"qtok{i}", [128, D], BF16) for i in range(NT)]
        ksl = [ctx.sb(f"ksl{i}", [128, 128], BF16) for i in range(2)]
        vsl = [ctx.sb(f"vsl{i}", [128, D], BF16) for i in range(2)]

        qcall = [ctx.sb(f"qcall{i}", [64, 2048], BF16) for i in range(2)]
        kwin = [ctx.sb(f"kwin{i}", [64, 2048], BF16) for i in range(2)]
        vcall = [ctx.sb(f"vcall{i}", [128, 64], BF16) for i in range(4)]
        msk_sb = ctx.sb("msk_sb", [128, 512])
        negmax = ctx.sb("negmax", [128, 4])
        pbf = ctx.sb("pbf", [128, 512], BF16)
        pt_sb = ctx.sb("pt_sb", [128, 512], BF16)
        rsinv = ctx.sb("rsinv", [128, 4])
        att_sb = [ctx.sb(f"attsb{i}", [128, 64], BF16) for i in range(2)]
        res_bf = [ctx.sb(f"resbf{i}", [128, D], BF16) for i in range(2)]
        x1_sb = [ctx.sb(f"x1sb{i}", [128, D]) for i in range(2)]
        zg = ctx.sb("zguard", [64, 128], BF16)

        scratch = {k: ctx.sb("ln_" + k, [128, 1]) for k in
                   ["sum", "msq", "mu", "var", "rstd", "nb", "m2", "eps"]}
        scratch["sq"] = ctx.sb("ln_sq", [128, D])

        ps_tp = ctx.ps("ps_tp", [128, 512], BF16)
        ps_q = ctx.ps("ps_q", [128, 512])
        ps_k = ctx.ps("ps_k", [128, 512])
        ps_v = ctx.ps("ps_v", [128, 512])
        ps_g = ctx.ps("ps_g", [64, 512])
        ps_sc = ctx.ps("ps_sc", [128, 512])
        ps_pt = ctx.ps("ps_pt", [128, 512], BF16)
        ps_val = ctx.ps("ps_val", [128, 320])
        ps_rs = ps_val

        consts = []
        for i in range(8):
            consts.append(p.add("gpsimd", lambda e, i=i: e.dma_start(out=wq_sb[i][:], in_=wqT[i * 128:(i + 1) * 128, :])))
            consts.append(p.add("gpsimd", lambda e, i=i: e.dma_start(out=wk_sb[i][:], in_=wkT[i * 128:(i + 1) * 128, :])))
            consts.append(p.add("gpsimd", lambda e, i=i: e.dma_start(out=wv_sb[i][:], in_=wvT[i * 128:(i + 1) * 128, :])))
        for i in range(16):
            consts.append(p.add("sync", lambda e, i=i: e.dma_start(out=sA[i][:], in_=smatA[i])))
            consts.append(p.add("sync", lambda e, i=i: e.dma_start(out=sB[i][:], in_=smatB[i])))
        consts.append(p.add("sync", lambda e: e.dma_start(out=mask[:], in_=maskd[:])))
        consts.append(p.add("sync", lambda e: e.dma_start(out=ident[:], in_=identd[:])))
        consts.append(p.add("sync", lambda e: e.dma_start(out=bk_sb[:], in_=bk[:])))
        consts.append(p.add("gpsimd", lambda e: e.dma_start(out=bqr[:], in_=bq[:])))
        consts.append(p.add("gpsimd", lambda e: e.dma_start(out=bvr[:], in_=bv[:])))
        nz = p.add("vector", lambda e: (e.memset(ones1[:], 1.0), e.memset(onescol[:], 1.0),
                                        e.memset(scratch["eps"][:], 1e-5),
                                        e.memset(zg[:], 0.0))[-1])
        consts.append(nz)
        kg2_guards = []
        for w in range(16):
            kg2_guards.append(p.add("sync", lambda e, w=w: e.dma_start(
                out=kg2[0:64, w * TPAD + 2176: w * TPAD + 2304], in_=zg[:]), [nz]))
        consts.extend(kg2_guards)

        def bb(e, dst, src, half):
            return e.matmul(dst[:, 0:512], ones1[:], src[0:1, half * 512:(half + 1) * 512],
                            start=True, stop=True)
        nb1 = p.add("tensor", lambda e: (bb(e, ps_q, bqr, 0), bb(e, ps_v, bvr, 0))[-1], consts)
        nb1e = p.add("vector", lambda e: (e.tensor_copy(bq_bc[:, 0:512], ps_q[:, 0:512]),
                                          e.tensor_copy(bv_bc[:, 0:512], ps_v[:, 0:512]))[-1], [nb1])
        nb2 = p.add("tensor", lambda e: (bb(e, ps_q, bqr, 1), bb(e, ps_v, bvr, 1))[-1], [nb1e])
        nb2e = p.add("vector", lambda e: (e.tensor_copy(bq_bc[:, 512:1024], ps_q[:, 0:512]),
                                          e.tensor_copy(bv_bc[:, 512:1024], ps_v[:, 0:512]))[-1], [nb2])
        cready = consts + [nb2e]

        prev_k_dma = {}
        all_k_dmas = []
        all_v_dmas = []
        prev_vd_buf = [None, None]
        tile_done_q = [None] * NT
        last_ln = [None, None]
        prev_tr = None
        for it in range(17):
            buf = it % 2
            xl = p.add("sync", lambda e, it=it, buf=buf: e.dma_start(
                out=xtile[buf][:], in_=xin[it * 128:(it + 1) * 128, :]),
                [last_ln[buf]] if last_ln[buf] else [])
            ln_deps = [xl, prev_tr] + (cready if it == 0 else [])
            ln = emit_ln_tile(ctx, xtile[buf][:], xn_bf[buf], scratch, ln_deps)
            last_ln[buf] = ln
            tevs = emit_transposes(ctx, xn_bf[buf], xnT, ident, ps_tp, ln, prev_round=prev_tr)
            tdep = tevs[-1]
            prev_tr = tdep

            kevs_last = None
            for cg in range(0, 8, 4):
                def kmm(e, cg=cg):
                    last = None
                    for j in range(4):
                        ci = cg + j
                        for kc in range(8):
                            last = e.matmul(ps_k[:, j * 128:(j + 1) * 128],
                                            wk_sb[kc][:, ci * 128:(ci + 1) * 128],
                                            xnT[kc][:], start=(kc == 0), stop=(kc == 7))
                    return last
                kn = p.add("tensor", kmm, [tdep] + ([kevs_last] if kevs_last else []))
                grp = []
                for j in range(4):
                    ci = cg + j
                    sl = j % 2
                    dep_slot = grp[-3] if len(grp) >= 4 else prev_k_dma.get(sl)
                    ke = p.add("vector", lambda e, j=j, ci=ci, sl=sl: e.tensor_scalar(
                        ksl[sl][:], ps_k[:, j * 128:(j + 1) * 128],
                        bk_sb[:, ci:ci + 1], None, ALU.add), [kn, dep_slot])
                    kd = p.add("gpsimd", lambda e, ci=ci, sl=sl, it=it: e.dma_start(
                        out=AP(kg2.tensor, (2 * ci) * TPAD + it * 128,
                               [[TPAD, 2], [16 * TPAD, 64], [1, 128]]),
                        in_=ksl[sl][:]), [ke])
                    grp += [ke, kd]
                    prev_k_dma[sl] = kd
                    all_k_dmas.append(kd)
                kevs_last = grp[-1]

            vdeps = [tdep, nb2e]
            vn0 = p.add("tensor", lambda e: [e.matmul(ps_v[:, 0:512], xnT[kc][:],
                        wv_sb[kc][:, 0:512], start=(kc == 0), stop=(kc == 7)) for kc in range(8)][-1],
                        vdeps)
            ve0 = p.add("vector", lambda e, buf=buf: e.scalar_tensor_tensor(
                vsl[buf][:, 0:512], ps_v[:, 0:512], 1.0, bv_bc[:, 0:512], ALU.mult, ALU.add),
                [vn0, prev_vd_buf[buf]])
            vn1 = p.add("tensor", lambda e: [e.matmul(ps_v[:, 0:512], xnT[kc][:],
                        wv_sb[kc][:, 512:1024], start=(kc == 0), stop=(kc == 7)) for kc in range(8)][-1],
                        [ve0])
            ve1 = p.add("vector", lambda e, buf=buf: e.scalar_tensor_tensor(
                vsl[buf][:, 512:1024], ps_v[:, 0:512], 1.0, bv_bc[:, 512:1024], ALU.mult, ALU.add), [vn1])
            vd = p.add("gpsimd", lambda e, buf=buf, it=it: e.dma_start(
                out=vdram[it * 128:(it + 1) * 128, :], in_=vsl[buf][:]), [ve1])
            prev_vd_buf[buf] = vd
            all_v_dmas.append(vd)

            if it >= 1:
                qt = it - 1
                qdeps = [tdep, nb2e] + ([tile_done_q[qt - 1]] if qt >= 1 else [])
                qn0 = p.add("tensor", lambda e: [e.matmul(ps_q[:, 0:512], xnT[kc][:],
                            wq_sb[kc][:, 0:512], start=(kc == 0), stop=(kc == 7)) for kc in range(8)][-1],
                            qdeps)
                qe0 = p.add("vector", lambda e, qt=qt: e.scalar_tensor_tensor(
                    q_tok[qt][:, 0:512], ps_q[:, 0:512], 1.0, bq_bc[:, 0:512], ALU.mult, ALU.add), [qn0])
                qn1 = p.add("tensor", lambda e: [e.matmul(ps_q[:, 0:512], xnT[kc][:],
                            wq_sb[kc][:, 512:1024], start=(kc == 0), stop=(kc == 7)) for kc in range(8)][-1],
                            [qe0])
                qe1 = p.add("vector", lambda e, qt=qt: e.scalar_tensor_tensor(
                    q_tok[qt][:, 512:1024], ps_q[:, 0:512], 1.0, bq_bc[:, 512:1024], ALU.mult, ALU.add), [qn1])
                if dbg_qdump:
                    qe1 = p.add("sync", lambda e, qt=qt: e.dma_start(
                        out=qdump[qt * 128:(qt + 1) * 128, :], in_=q_tok[qt][:]), [qe1])
                tile_done_q[qt] = qe1

        prev_gather = None
        prev_vn = None
        prev_en = None
        prev_last_ae = None
        prev_scatter = []
        prev_vcall = [None] * 4
        prev_kw = [None, None]
        for k in range(NW):
            wb = k % 2
            calls = [c for c in range(16 * k - 14, 16 * k + 2) if 0 <= c < NCALLS]
            kw = p.add("gpsimd", lambda e, k=k, wb=wb: e.dma_start(
                out=kwin[wb][:].rearrange("p (w u) -> p w u", w=16),
                in_=AP(kg2.tensor, 128 * k + 1, [[16 * TPAD, 64], [TPAD, 16], [1, 128]])),
                p.dma_barrier(all_k_dmas) + p.dma_barrier(kg2_guards) +
                ([prev_kw[wb]] if prev_kw[wb] else []))
            gevs = []
            for jg in range(0, 16, 4):
                def gmm(e, jg=jg, k=k):
                    last = None
                    for jj in range(4):
                        jt = jg + jj
                        h = 15 - jt
                        first = True
                        if k >= 1:
                            last = e.matmul(ps_g[:, jj * 128:(jj + 1) * 128],
                                            q_tok[k - 1][:, h * 64:(h + 1) * 64], sA[jt][:],
                                            start=True, stop=(k == 16))
                            first = False
                        if k <= 15:
                            last = e.matmul(ps_g[:, jj * 128:(jj + 1) * 128],
                                            q_tok[k][:, h * 64:(h + 1) * 64], sB[jt][:],
                                            start=first, stop=True)
                    return last
                qdeps = [tile_done_q[k - 1] if k >= 1 else None,
                         tile_done_q[k] if k <= 15 else None]
                gn = p.add("tensor", gmm,
                           qdeps + ([gevs[-1]] if gevs else ([prev_gather] if prev_gather else [])))

                def gev(e, jg=jg, wb=wb):
                    last = None
                    for jj in range(4):
                        jt = jg + jj
                        last = e.tensor_copy(
                            AP(qcall[wb], jt * 8, [[2048, 64], [128, 16], [1, 8]]),
                            ps_g[:, jj * 128:(jj + 1) * 128].rearrange("p (c g) -> p c g", g=8))
                    return last
                gevs.append(p.add("vector", gev, [gn]))
            prev_gather = gevs[-1]
            qc_ready = gevs[-1]

            grp_last_vn = None
            for gi in range(0, len(calls), 4):
                grp = calls[gi:gi + 4]
                ng = len(grp)

                def smm(e, grp=grp, k=k, wb=wb):
                    last = None
                    for j, c in enumerate(grp):
                        c_rel = c - (16 * k - 14)
                        last = e.matmul(
                            ps_sc[:, j * 128:(j + 1) * 128],
                            qcall[wb][:, c_rel * 128:(c_rel + 1) * 128],
                            AP(kwin[wb], 8 * c_rel, [[2048, 64], [1, 8], [128, 16]]),
                            start=True, stop=True)
                    return last
                sn = p.add("tensor", smm,
                           [qc_ready, kw] + ([grp_last_vn or prev_vn] if (grp_last_vn or prev_vn) else []))
                vloads = []
                for j, c in enumerate(grp):
                    vl = p.add("gpsimd", lambda e, c=c, j=j: e.dma_start(
                        out=vcall[j][:],
                        in_=AP(vdram.tensor, (8 * c + 113) * D, [[D, 8], [64, 16], [1, 64]])),
                        p.dma_barrier(all_v_dmas) + [grp_last_vn or prev_vn] +
                        ([prev_vcall[j]] if prev_vcall[j] else []))
                    vloads.append(vl)
                    prev_vcall[j] = vl

                def mstt(e, ng=ng):
                    for j in range(ng):
                        e.scalar_tensor_tensor(msk_sb[:, j * 128:(j + 1) * 128],
                                               ps_sc[:, j * 128:(j + 1) * 128], 0.125,
                                               mask[:], ALU.mult, ALU.add)
                    return e.reduce_max(negmax[:, 0:ng],
                                        msk_sb[:, 0:ng * 128].rearrange("p (c n) -> p c n", n=128),
                                        axis=AX.X, negate=True)
                mn = p.add("vector", mstt, [sn] + ([prev_en] if prev_en else []))

                def expn(e, ng=ng):
                    last = None
                    for j in range(ng):
                        last = e.activation(pbf[:, j * 128:(j + 1) * 128],
                                            msk_sb[:, j * 128:(j + 1) * 128], AF.Exp,
                                            bias=negmax[:, j:j + 1])
                    return last
                en = p.add("scalar", expn, [mn])
                prev_en = en

                def ptr(e, ng=ng):
                    last = None
                    for j in range(ng):
                        last = e.transpose(ps_pt[:, j * 128:(j + 1) * 128],
                                           pbf[:, j * 128:(j + 1) * 128], ident[:])
                    return last
                tn = p.add("tensor", ptr, [en])
                pte = p.add("vector", lambda e, ng=ng: e.tensor_copy(pt_sb[:, 0:ng * 128],
                                                                     ps_pt[:, 0:ng * 128]), [tn])

                def rsv(e, ng=ng):
                    last = None
                    for j in range(ng):
                        e.matmul(ps_rs[:, 256 + j:257 + j], pt_sb[:, j * 128:(j + 1) * 128],
                                 onescol[:], start=True, stop=True)
                        last = e.matmul(ps_val[:, j * 64:(j + 1) * 64],
                                        pt_sb[:, j * 128:(j + 1) * 128], vcall[j][:],
                                        start=True, stop=True)
                    return last
                vn = p.add("tensor", rsv, [pte] + vloads +
                           ([prev_last_ae] if prev_last_ae else []))
                rn = p.add("vector", lambda e: e.reciprocal(rsinv[:], ps_rs[:, 256:260]),
                           [vn] + ([prev_last_ae] if prev_last_ae else []))
                for j, c in enumerate(grp):
                    ab = j % 2
                    ae = p.add("scalar", lambda e, j=j, ab=ab: e.activation(
                        att_sb[ab][:], ps_val[:, j * 64:(j + 1) * 64], AF.Copy,
                        scale=rsinv[:, j:j + 1]), [rn] +
                        ([prev_scatter[-2]] if len(prev_scatter) >= 2 else []))
                    sc = p.add("sync", lambda e, c=c, ab=ab: e.dma_start(
                        out=AP(attnd.tensor, (8 * c + 113) * D + 960,
                               [[D - 64, 16], [D, 8], [1, 64]]),
                        in_=att_sb[ab][:]), [ae])
                    prev_scatter.append(sc)
                    prev_last_ae = ae
                grp_last_vn = vn
            prev_vn = grp_last_vn or prev_vn
            prev_kw[wb] = grp_last_vn or kw

        store_nodes = []
        prev_x1 = [None, None]
        for it in range(NT):
            buf = it % 2
            rl = p.add("sync", lambda e, it=it, buf=buf: e.dma_start(
                out=res_bf[buf][:], in_=attnd[128 + it * 128:128 + (it + 1) * 128, :]),
                p.dma_barrier(prev_scatter) + ([prev_x1[buf]] if prev_x1[buf] else []))
            xl = p.add("sync", lambda e, it=it, buf=buf: e.dma_start(
                out=x1_sb[buf][:], in_=xin[128 + it * 128:128 + (it + 1) * 128, :]), [rl])
            ad = p.add("vector", lambda e, buf=buf: e.tensor_tensor(
                x1_sb[buf][:], x1_sb[buf][:], res_bf[buf][:], ALU.add), [rl, xl])
            st = p.add("sync", lambda e, it=it, buf=buf: e.dma_start(
                out=x1out[it * 128:(it + 1) * 128, :], in_=x1_sb[buf][:]), [ad])
            prev_x1[buf] = st
            store_nodes.append(st)

        return ctx.finish(store_nodes)


def host_l0a_inputs(x_full, Wq, bq_, Wk, bk_, Wv, bv_):
    smatA = np.zeros((16, 128, 128), dtype=BF)
    smatB = np.zeros((16, 128, 128), dtype=BF)
    for jt in range(16):
        for cr in range(16):
            for g in range(8):
                tl = 8 * cr + 1 + g + jt
                if 0 <= tl < 128:
                    smatA[jt, tl, cr * 8 + g] = 1
                if 0 <= tl - 128 < 128:
                    smatB[jt, tl - 128, cr * 8 + g] = 1
    mask = np.full((128, 128), -1e30, dtype=np.float32)
    for m in range(128):
        for n in range(128):
            if m % 8 == n // 16:
                mask[m, n] = 0.0
    ident = np.eye(128, dtype=BF)
    wqT = np.ascontiguousarray(np.asarray(Wq).T).astype(np.float32)
    wkT = np.ascontiguousarray(np.asarray(Wk).T).astype(np.float32)
    wvT = np.ascontiguousarray(np.asarray(Wv).T).astype(np.float32)
    bkr = np.ascontiguousarray(np.asarray(bk_).reshape(8, 128).T).astype(np.float32)
    x_full = np.asarray(x_full)
    in_maps = []
    for core in range(NCORES):
        b, s0 = core // 4, (core % 4) * T
        xin = np.zeros((2176, D), dtype=np.float32)
        xin[128:, :] = x_full[b, s0:s0 + T]
        lo = max(0, s0 - 15)
        nh = s0 - lo
        if nh > 0:
            xin[128 - nh:128, :] = x_full[b, lo:s0]
        in_maps.append(dict(xin=xin, wqT=wqT, wkT=wkT, wvT=wvT,
                            bq=np.asarray(bq_).reshape(1, D).astype(np.float32),
                            bv=np.asarray(bv_).reshape(1, D).astype(np.float32), bk=bkr,
                            smatA=smatA, smatB=smatB, mask=mask, ident=ident))
    return in_maps



SPIKE_BPS = [0.24419428096993973, 0.4639691338428854, 0.6861859295255306,
             0.9059607823984763, 1.0839784132255623]


def build_l0f(debug=False):
    with Ctx("l0f") as ctx:
        nc, p = ctx.nc, ctx.prog
        x1in = ctx.din("x1in", [T, D])
        ydump = ctx.dout("ydump", [T, D], BF16) if debug else None
        wfcT = ctx.din("wfcT", [D, 4 * D])
        wpjT = ctx.din("wpjT", [4 * D, D])
        identd = ctx.din("ident", [128, 128], BF16)
        x2out = ctx.dout("x2", [T, D])

        wfc_sb = [ctx.sb(f"wfc{i}", [128, 4 * D], BF16) for i in range(8)]
        wpj_sb = [ctx.sb(f"wpj{i}", [128, D], BF16) for i in range(32)]
        ident = ctx.sb("ident_sb", [128, 128], BF16)
        xtile = [ctx.sb(f"xt{i}", [128, D]) for i in range(2)]
        xn_bf = [ctx.sb(f"xnbf{i}", [128, D], BF16) for i in range(2)]
        xnT = [ctx.sb(f"xnT{i}", [128, 256], BF16) for i in range(8)]
        relu_sb = [ctx.sb(f"relu{i}", [128, 256]) for i in range(2)]
        hsq = [ctx.sb(f"hsq{i}", [128, 256], BF16) for i in range(2)]
        y_bf = ctx.sb("y_bf", [128, D], BF16)
        acc = [ctx.sb(f"acc{i}", [128, D], BF16) for i in range(2)]
        x2_sb = [ctx.sb(f"x2sb{i}", [128, D]) for i in range(2)]
        scratch = {k: ctx.sb("ln_" + k, [128, 1]) for k in
                   ["sum", "msq", "mu", "var", "rstd", "nb", "m2", "eps"]}
        scratch["sq"] = ctx.sb("ln_sq", [128, D])

        ps_tp = ctx.ps("ps_tp", [128, 512], BF16)
        ps_h = ctx.ps("ps_h", [128, 512])
        ps_y = [ctx.ps(f"ps_y{i}", [128, 512]) for i in range(4)]

        consts = []
        for i in range(8):
            consts.append(p.add("gpsimd", lambda e, i=i: e.dma_start(
                out=wfc_sb[i][:], in_=wfcT[i * 128:(i + 1) * 128, :])))
        for i in range(32):
            consts.append(p.add("gpsimd", lambda e, i=i: e.dma_start(
                out=wpj_sb[i][:], in_=wpjT[i * 128:(i + 1) * 128, :])))
        consts.append(p.add("sync", lambda e: e.dma_start(out=ident[:], in_=identd[:])))
        consts.append(p.add("vector", lambda e: e.memset(scratch["eps"][:], 1e-5)))

        store_nodes = []
        last_use_xtile = [None, None]
        last_use_x2 = [None, None]
        prev_tr = None
        prev_relu = [None, None]
        prev_hsq_reader = [None, None]
        prev_pair_ev = None
        for tp in range(8):
            tile_lns = []
            for j in range(2):
                it = 2 * tp + j
                buf = j
                xl = p.add("sync", lambda e, it=it, buf=buf: e.dma_start(
                    out=xtile[buf][:], in_=x1in[it * 128:(it + 1) * 128, :]),
                    [last_use_xtile[buf]])
                ln_deps = [xl, prev_tr] + (consts if it == 0 else [])
                ln = emit_ln_tile(ctx, xtile[buf][:], xn_bf[buf], scratch, ln_deps)
                tevs = emit_transposes(ctx, xn_bf[buf], xnT, ident, ps_tp, ln,
                                       col0=j * 128, prev_round=prev_tr)
                prev_tr = tevs[-1]
                tile_lns.append(ln)
            tdep = prev_tr

            y_open = [False] * 4
            prev_ymm = None
            for f4 in range(32):
                sl = f4 % 2
                def hmm(e, f4=f4, sl=sl):
                    last = None
                    for kc in range(8):
                        last = e.matmul(ps_h[:, sl * 256:sl * 256 + 256],
                                        wfc_sb[kc][:, f4 * 128:(f4 + 1) * 128],
                                        xnT[kc][:], start=(kc == 0), stop=(kc == 7))
                    return last
                hn = p.add("tensor", hmm, [tdep] + ([prev_relu[sl]] if prev_relu[sl] else [])
                           + ([prev_ymm] if prev_ymm else []))
                rn = p.add("scalar", lambda e, sl=sl: e.activation(
                    relu_sb[sl][:], ps_h[:, sl * 256:sl * 256 + 256], AF.Relu),
                    [hn] + ([prev_hsq_reader[sl]] if prev_hsq_reader[sl] else []))
                qn = p.add("vector", lambda e, sl=sl: e.tensor_tensor(
                    hsq[sl][:], relu_sb[sl][:], relu_sb[sl][:], ALU.mult), [rn])
                def ymm(e, f4=f4, sl=sl):
                    last = None
                    for j in range(2):
                        for h in range(2):
                            last = e.matmul(ps_y[2 * j + h][:],
                                            hsq[sl][:, j * 128:(j + 1) * 128],
                                            wpj_sb[f4][:, h * 512:(h + 1) * 512],
                                            start=(f4 == 0), stop=(f4 == 31))
                    return last
                yn = p.add("tensor", ymm, [qn] + ([prev_pair_ev] if f4 == 0 else []))
                prev_hsq_reader[sl] = yn
                prev_relu[sl] = qn
                prev_ymm = yn

            for j in range(2):
                it = 2 * tp + j
                buf = j
                ev = p.add("scalar", lambda e, j=j: (
                    e.activation(y_bf[:, 0:512], ps_y[2 * j][:], AF.Copy),
                    e.activation(y_bf[:, 512:1024], ps_y[2 * j + 1][:], AF.Copy))[-1],
                    [prev_ymm] + store_nodes[-1:])
                s0 = p.add("vector", lambda e: e.tensor_scalar(
                    acc[0][:], y_bf[:], SPIKE_BPS[0], None, ALU.is_gt), [ev])
                cur = 0
                sn = s0
                for bi in range(1, 5):
                    nxt = 1 - cur
                    sn = p.add("vector", lambda e, bi=bi, cur=cur, nxt=nxt: e.scalar_tensor_tensor(
                        acc[nxt][:], y_bf[:], SPIKE_BPS[bi], acc[cur][:],
                        ALU.is_gt, ALU.add), [sn])
                    cur = nxt
                ad = p.add("vector", lambda e, buf=buf, cur=cur: e.scalar_tensor_tensor(
                    x2_sb[buf][:], acc[cur][:], 0.2, xtile[buf][:], ALU.mult, ALU.add),
                    [sn, last_use_x2[buf]])
                last_use_xtile[buf] = ad
                if debug:
                    p.add("sync", lambda e, it=it: e.dma_start(
                        out=ydump[it * 128:(it + 1) * 128, :], in_=y_bf[:]), [s0])
                st = p.add("sync", lambda e, it=it, buf=buf: e.dma_start(
                    out=x2out[it * 128:(it + 1) * 128, :], in_=x2_sb[buf][:]), [ad])
                last_use_x2[buf] = st
                store_nodes.append(st)
                prev_pair_ev = ev

        return ctx.finish(store_nodes)


def host_l0f_inputs(x1_cores, W_fc, W_proj):
    wfcT = np.ascontiguousarray(np.asarray(W_fc).T).astype(np.float32)
    wpjT = np.ascontiguousarray(np.asarray(W_proj).T).astype(np.float32)
    ident = np.eye(128, dtype=BF)
    return [dict(x1in=x1_cores[c], wfcT=wfcT, wpjT=wpjT, ident=ident)
            for c in range(NCORES)]



GATE_SCALE = 1.0 / 16.0
QSCALE_LN = float(np.log(128.0 ** -0.5))
NCH = 32


def build_l1a_p1():
    with Ctx("l1a1") as ctx:
        nc, p = ctx.nc, ctx.prog
        x2in = ctx.din("x2in", [T, D])
        wk_d = ctx.din("wk", [D, 512])
        wv_d = ctx.din("wv", [D, D])
        wgk1_d = ctx.din("wgk1", [D, 16])
        wgk2_d = ctx.din("wgk2", [16, 512])
        bgk2_d = ctx.din("bgk2", [1, 512])
        identd = ctx.din("ident", [128, 128], BF16)
        identfd = ctx.din("identf", [128, 128])
        slocout = ctx.dout("sloc", [128, 1024])
        dtotout = ctx.dout("dtot", [128, 4])
        kTd = ctx.dout("kTd", [512, T], BF16)
        vd = ctx.dout("vd", [T, D], BF16)
        gd = ctx.dout("gd", [512, T])

        wk_sb = [ctx.sb(f"wk{i}", [128, 512], BF16) for i in range(8)]
        wv_sb = [ctx.sb(f"wv{i}", [128, D], BF16) for i in range(8)]
        wgk1_sb = [ctx.sb(f"wg1{i}", [128, 16], BF16) for i in range(8)]
        wgk2_sb = ctx.sb("wg2", [16, 512], BF16)
        bgk2_bc = ctx.sb("bgk2bc", [128, 512])
        bgk2_sb = ctx.sb("bgk2r", [1, 512], BF16)
        ident = ctx.sb("ident_sb", [128, 128], BF16)
        identf = ctx.sb("identf_sb", [128, 128])
        ones1 = ctx.sb("ones1", [1, 128], BF16)
        onesf = ctx.sb("onesf", [128, 1])
        zeros64 = ctx.sb("zeros64", [128, 64])
        xtile = [ctx.sb(f"xt{i}", [128, D]) for i in range(2)]
        xn_bf = [ctx.sb(f"xnbf{i}", [128, D], BF16) for i in range(2)]
        xnT = [ctx.sb(f"xnT{i}", [128, 128], BF16) for i in range(8)]
        kT_sb = [ctx.sb(f"kTs{i}", [128, 128], BF16) for i in range(4)]
        v_sb = [ctx.sb(f"vsb{i}", [128, D], BF16) for i in range(2)]
        rT_sb = ctx.sb("rTs", [16, 128], BF16)
        gk_abs = ctx.sb("gkabs", [128, 512])
        gk_e = ctx.sb("gke", [128, 512])
        gk_l = ctx.sb("gkl", [128, 512])
        gk_min = ctx.sb("gkmin", [128, 512])
        gk_tok = ctx.sb("gktok", [128, 512])
        G_sb = ctx.sb("G_sb", [128, 512])
        glall = ctx.sb("glall", [128, 128])
        gl16 = ctx.sb("gl16", [128, 8])
        dcat = ctx.sb("dcat", [128, 8])
        kgf = ctx.sb("kgf", [128, 512], BF16)
        kg_sb = ctx.sb("kg_sb", [128, 512], BF16)
        kgtok = ctx.sb("kgtok", [128, 512], BF16)
        S_sb = ctx.sb("S_sb", [128, 1024])
        gsum = ctx.sb("gsum", [128, 4])
        scratch = {k: ctx.sb("ln_" + k, [128, 1]) for k in
                   ["sum", "msq", "mu", "var", "rstd", "nb", "m2", "eps"]}
        scratch["sq"] = ctx.sb("ln_sq", [128, D])

        ps_tp = ctx.ps("ps_tp", [128, 512], BF16)
        ps_k = ctx.ps("ps_k", [128, 512])
        ps_v = ctx.ps("ps_v", [128, 512])
        ps_r = ctx.ps("ps_r", [16, 128])
        ps_u = ctx.ps("ps_u", [128, 512])
        ps_gkT = ctx.ps("ps_gkT", [128, 512])
        ps_U = ctx.ps("ps_U", [128, 256])

        consts = []
        for i in range(8):
            consts.append(p.add("gpsimd", lambda e, i=i: e.dma_start(out=wk_sb[i][:], in_=wk_d[i * 128:(i + 1) * 128, :])))
            consts.append(p.add("gpsimd", lambda e, i=i: e.dma_start(out=wv_sb[i][:], in_=wv_d[i * 128:(i + 1) * 128, :])))
            consts.append(p.add("gpsimd", lambda e, i=i: e.dma_start(out=wgk1_sb[i][:], in_=wgk1_d[i * 128:(i + 1) * 128, :])))
        consts.append(p.add("gpsimd", lambda e: e.dma_start(out=wgk2_sb[:], in_=wgk2_d[:])))
        consts.append(p.add("gpsimd", lambda e: e.dma_start(out=bgk2_sb[:], in_=bgk2_d[:])))
        consts.append(p.add("sync", lambda e: e.dma_start(out=ident[:], in_=identd[:])))
        consts.append(p.add("sync", lambda e: e.dma_start(out=identf[:], in_=identfd[:])))
        nz = p.add("vector", lambda e: (e.memset(ones1[:], 1.0), e.memset(onesf[:], 1.0),
                                        e.memset(zeros64[:], 0.0), e.memset(S_sb[:], 0.0),
                                        e.memset(gsum[:], 0.0), e.memset(glall[:], 0.0),
                                        e.memset(scratch["eps"][:], 1e-5))[-1])
        consts.append(nz)
        nb1 = p.add("tensor", lambda e: e.matmul(ps_u[:, 0:512], ones1[:], bgk2_sb[0:1, :],
                                                 start=True, stop=True), consts)
        nb1e = p.add("vector", lambda e: e.tensor_copy(bgk2_bc[:], ps_u[:, 0:512]), [nb1])
        cready = consts + [nb1e]

        store_nodes = []
        last_ln = [None, None]
        prev_tr = None
        prev_vsb = [[], []]
        prev_kT_reader = None
        prev_gk_cycle = None
        prev_Supd = [None] * 4
        prev_su_any = None
        prev_ab = None
        all_spills = []
        for it in range(16):
            buf = it % 2
            xl = p.add("sync", lambda e, it=it, buf=buf: e.dma_start(
                out=xtile[buf][:], in_=x2in[it * 128:(it + 1) * 128, :]),
                [last_ln[buf]])
            ln_deps = [xl, prev_tr] + (cready if it == 0 else [])
            ln = emit_ln_tile(ctx, xtile[buf][:], xn_bf[buf], scratch, ln_deps)
            last_ln[buf] = ln
            tevs = emit_transposes(ctx, xn_bf[buf], xnT, ident, ps_tp, ln, prev_round=prev_tr)
            tdep = tevs[-1]
            prev_tr = tdep

            def kmm(e):
                last = None
                for h in range(4):
                    for kc in range(8):
                        last = e.matmul(ps_k[:, h * 128:(h + 1) * 128],
                                        wk_sb[kc][:, h * 128:(h + 1) * 128],
                                        xnT[kc][:], start=(kc == 0), stop=(kc == 7))
                return last
            kn = p.add("tensor", kmm, [tdep] + ([prev_kT_reader] if prev_kT_reader else []))
            kev = p.add("vector", lambda e: [e.tensor_copy(kT_sb[h][:], ps_k[:, h * 128:(h + 1) * 128])
                                             for h in range(4)][-1], [kn])
            kspills = []
            for h in range(4):
                kspills.append(p.add("gpsimd", lambda e, h=h, it=it: e.dma_start(
                    out=kTd[h * 128:(h + 1) * 128, it * 128:(it + 1) * 128],
                    in_=kT_sb[h][:]), [kev]))

            vn0 = p.add("tensor", lambda e: [e.matmul(ps_v[:, 0:512], xnT[kc][:],
                        wv_sb[kc][:, 0:512], start=(kc == 0), stop=(kc == 7)) for kc in range(8)][-1],
                        [tdep])
            ve0 = p.add("vector", lambda e, buf=buf: e.tensor_copy(v_sb[buf][:, 0:512], ps_v[:, 0:512]),
                        [vn0] + prev_vsb[buf])
            vn1 = p.add("tensor", lambda e: [e.matmul(ps_v[:, 0:512], xnT[kc][:],
                        wv_sb[kc][:, 512:1024], start=(kc == 0), stop=(kc == 7)) for kc in range(8)][-1],
                        [ve0])
            ve1 = p.add("vector", lambda e, buf=buf: e.tensor_copy(v_sb[buf][:, 512:1024], ps_v[:, 0:512]),
                        [vn1])
            vsp = p.add("gpsimd", lambda e, buf=buf, it=it: e.dma_start(
                out=vd[it * 128:(it + 1) * 128, :], in_=v_sb[buf][:]), [ve1])
            all_spills.append(vsp)

            rn = p.add("tensor", lambda e: [e.matmul(ps_r[:], wgk1_sb[kc][:], xnT[kc][:],
                       start=(kc == 0), stop=(kc == 7)) for kc in range(8)][-1],
                       [tdep] + ([prev_gk_cycle] if prev_gk_cycle else []))
            rev = p.add("vector", lambda e: e.tensor_copy(rT_sb[:], ps_r[:]), [rn])
            un = p.add("tensor", lambda e: e.matmul(ps_u[:, 0:512], rT_sb[:], wgk2_sb[:],
                                                    start=True, stop=True),
                       [rev, nb1e] + ([prev_ab] if prev_ab else []))
            ab = p.add("vector", lambda e: e.scalar_tensor_tensor(
                gk_abs[:], ps_u[:, 0:512], 1.0, bgk2_bc[:], ALU.mult, ALU.add), [un])
            prev_ab = ab
            a1 = p.add("scalar", lambda e: e.activation(gk_e[:], gk_abs[:], AF.Abs), [ab])
            a2 = p.add("vector", lambda e: e.tensor_scalar(gk_min[:], gk_abs[:], 0.0, None, ALU.min), [a1])
            a3 = p.add("scalar", lambda e: e.activation(gk_l[:], gk_e[:], AF.Exp, scale=-1.0), [a2])
            a4 = p.add("scalar", lambda e: e.activation(gk_e[:], gk_l[:], AF.Ln,
                                                        bias=onesf[:, 0:1]), [a3])
            a5 = p.add("vector", lambda e: e.tensor_tensor(gk_tok[:], gk_min[:], gk_e[:],
                                                           ALU.subtract), [a4])
            def gtr(e):
                last = None
                for s in range(4):
                    last = e.transpose(ps_gkT[:, s * 128:(s + 1) * 128],
                                       gk_tok[:, s * 128:(s + 1) * 128], identf[:])
                return last
            gt = p.add("tensor", gtr, [a5])
            def scans(e):
                last = None
                for s in range(4):
                    for ch in range(2):
                        c0 = s * 128 + ch * 64
                        last = e.tensor_tensor_scan(G_sb[:, c0:c0 + 64],
                                                    ps_gkT[:, c0:c0 + 64], zeros64[:, 0:64],
                                                    0.0, ALU.add, ALU.add)
                return last
            sc = p.add("vector", scans, [gt])
            prev_gk_cycle = sc
            gsp = p.add("gpsimd", lambda e, it=it: e.dma_start(
                out=AP(gd.tensor, it * 128, [[T, 128], [128 * T, 4], [1, 128]]),
                in_=G_sb[:].rearrange("p (s t) -> p s t", s=4)), [sc])
            all_spills.append(gsp)
            glc = p.add("vector", lambda e, it=it: e.tensor_copy(
                AP(glall, it * 2, [[128, 128], [32, 4], [1, 2]]),
                AP(G_sb, 63, [[512, 128], [128, 4], [64, 2]])), [sc])
            g16 = p.add("scalar", lambda e, it=it: e.activation(
                gl16[:], AP(glall, it * 2, [[128, 128], [32, 4], [1, 2]]), AF.Copy,
                scale=GATE_SCALE), [glc])
            dn = p.add("vector", lambda e: e.tensor_copy(dcat[:], gl16[:]), [g16])
            dn2 = p.add("scalar", lambda e: e.activation(dcat[:], dcat[:], AF.Exp), [dn])
            def kgfn(e):
                last = None
                for h in range(4):
                    for ch in range(2):
                        c0 = h * 128 + ch * 64
                        last = e.activation(kgf[:, c0:c0 + 64], G_sb[:, c0:c0 + 64], AF.Exp,
                                            scale=-GATE_SCALE, bias=gl16[:, 2 * h + ch:2 * h + ch + 1])
                return last
            kf = p.add("scalar", kgfn, [dn2, kev])
            kgn = p.add("vector", lambda e: [e.tensor_tensor(kg_sb[:, h * 128:(h + 1) * 128],
                        kgf[:, h * 128:(h + 1) * 128], kT_sb[h][:], ALU.mult) for h in range(4)][-1],
                        [kf])
            def kgtr(e):
                last = None
                for h in range(4):
                    last = e.transpose(ps_tp[:, h * 128:(h + 1) * 128],
                                       kg_sb[:, h * 128:(h + 1) * 128], ident[:])
                return last
            kt = p.add("tensor", kgtr, [kgn, tdep])
            kte = p.add("vector", lambda e: e.tensor_copy(kgtok[:], ps_tp[:]), [kt])
            prev_kT_reader = kgn
            prev_tr = kte

            last_um = None
            for ch in range(2):
                for h in range(4):
                    um = p.add("tensor", lambda e, ch=ch, h=h: e.matmul(
                        ps_U[:], kgtok[ch * 64:(ch + 1) * 64, h * 128:(h + 1) * 128],
                        v_sb[buf][ch * 64:(ch + 1) * 64, h * 256:(h + 1) * 256],
                        start=True, stop=True), [kte, ve1] +
                        ([prev_su_any] if prev_su_any else []))
                    su = p.add("vector", lambda e, ch=ch, h=h, it=it: e.scalar_tensor_tensor(
                        S_sb[:, h * 256:(h + 1) * 256], S_sb[:, h * 256:(h + 1) * 256],
                        dcat[:, 2 * h + ch:2 * h + ch + 1], ps_U[:],
                        ALU.mult, ALU.add), [um])
                    prev_Supd[h] = su
                    prev_su_any = su
                    last_um = um

            prev_vsb[buf] = [last_um, vsp]

        s_st = p.add("sync", lambda e: e.dma_start(out=slocout[:], in_=S_sb[:]),
                     [n for n in prev_Supd if n])
        gr = p.add("vector", lambda e: e.reduce_sum(
            gsum[:], glall[:].rearrange("p (h c) -> p h c", h=4), axis=AX.X),
            [n for n in prev_Supd if n] + [glc])
        gr2 = p.add("scalar", lambda e: e.activation(gsum[:], gsum[:], AF.Exp,
                                                     scale=GATE_SCALE), [gr])
        d_st = p.add("sync", lambda e: e.dma_start(out=dtotout[:], in_=gsum[:]), [gr2])
        return ctx.finish([s_st, d_st] + p.dma_barrier(all_spills))


def build_l1a_p2():
    with Ctx("l1a2") as ctx:
        nc, p = ctx.nc, ctx.prog
        x2in = ctx.din("x2in", [T, D])
        kTd = ctx.din("kTd", [512, T], BF16)
        vd = ctx.din("vd", [T, D], BF16)
        gd = ctx.din("gd", [512, T])
        s1d = ctx.din("s1", [128, 1024])
        s2d = ctx.din("s2", [128, 1024])
        s3d = ctx.din("s3", [128, 1024])
        d1d = ctx.din("d1", [128, 4])
        d2d = ctx.din("d2", [128, 4])
        wq_d = ctx.din("wq", [D, 512])
        wg_d = ctx.din("wg", [D, D])
        wo_d = ctx.din("wo", [D, D])
        gnw_d = ctx.din("gnw", [1, D])
        tril_d = ctx.din("trilT", [64, 64])
        identd = ctx.din("ident", [128, 128], BF16)
        x3out = ctx.dout("x3", [T, D])

        wq_sb = [ctx.sb(f"wq{i}", [128, 512], BF16) for i in range(8)]
        wg_sb = [ctx.sb(f"wg{i}", [128, D], BF16) for i in range(8)]
        wo_sb = [ctx.sb(f"wo{i}", [128, D], BF16) for i in range(8)]
        gnwr = ctx.sb("gnwr", [1, D], BF16)
        gnwb = ctx.sb("gnwb", [128, D], BF16)
        tril = ctx.sb("tril", [64, 64])
        ident = ctx.sb("ident_sb", [128, 128], BF16)
        ones1 = ctx.sb("ones1", [1, 128], BF16)
        S_sb = ctx.sb("S_sb", [128, 1024])
        Sbf = ctx.sb("Sbf", [128, 1024], BF16)
        stmp = ctx.sb("stmp", [128, 1024])
        s1_sb = ctx.sb("s1_sb", [128, 1024])
        s2_sb = ctx.sb("s2_sb", [128, 1024])
        s3_sb = ctx.sb("s3_sb", [128, 1024])
        d1_sb = ctx.sb("d1_sb", [128, 4])
        d2_sb = ctx.sb("d2_sb", [128, 4])
        xtile = [ctx.sb(f"xt{i}", [128, D]) for i in range(2)]
        xn_bf = [ctx.sb(f"xnbf{i}", [128, D], BF16) for i in range(2)]
        xnT = [ctx.sb(f"xnT{i}", [128, 128], BF16) for i in range(8)]
        qT_t = ctx.sb("qT_t", [128, 512], BF16)
        kT_t = ctx.sb("kT_t", [128, 512], BF16)
        G_t = ctx.sb("G_t", [128, 512])
        gsil = ctx.sb("gsil", [128, D], BF16)
        gs_sb = ctx.sb("gs_sb", [128, D], BF16)
        expGs = ctx.sb("expGs", [128, 512], BF16)
        expnG = ctx.sb("expnG", [128, 512], BF16)
        gl16 = ctx.sb("gl16", [128, 8])
        dcat = ctx.sb("dcat", [128, 8])
        kgf = ctx.sb("kgf", [128, 512], BF16)
        qg_sb = ctx.sb("qg_sb", [128, 512], BF16)
        ki_sb = ctx.sb("ki_sb", [128, 512], BF16)
        kg_sb = ctx.sb("kg_sb", [128, 512], BF16)
        kgtok = ctx.sb("kgtok", [128, 512], BF16)
        vc_sb = [ctx.sb(f"vc{i}", [64, D], BF16) for i in range(2)]
        v_t = ctx.sb("v_t", [128, D], BF16)
        A_sb = [ctx.sb(f"A_sb{i}", [64, 64], BF16) for i in range(2)]
        oraw = ctx.sb("oraw", [128, D], BF16)
        osq = ctx.sb("osq", [128, D])
        ms_sb = ctx.sb("ms_sb", [128, 4])
        ms2_sb = ctx.sb("ms2_sb", [128, 4])
        rstd4 = ctx.sb("rstd4", [128, 4])
        gated = ctx.sb("gated", [128, D], BF16)
        x3_sb = [ctx.sb(f"x3sb{i}", [128, D]) for i in range(2)]
        qsc = ctx.sb("qsc", [128, 1])
        scratch = {k: ctx.sb("ln_" + k, [128, 1]) for k in
                   ["sum", "msq", "mu", "var", "rstd", "nb", "m2", "eps"]}
        scratch["sq"] = ctx.sb("ln_sq", [128, D])

        ps_tp = ctx.ps("ps_tp", [128, 512], BF16)
        ps_q = ctx.ps("ps_q", [128, 512])
        ps_g = ctx.ps("ps_g", [128, 512])
        ps_A = ctx.ps("ps_A", [64, 128])
        ps_U = ctx.ps("ps_U", [128, 256])
        ps_o = ctx.ps("ps_o", [64, 512])
        ps_x = ctx.ps("ps_x", [128, 512])

        consts = []
        for i in range(8):
            consts.append(p.add("gpsimd", lambda e, i=i: e.dma_start(out=wq_sb[i][:], in_=wq_d[i * 128:(i + 1) * 128, :])))
            consts.append(p.add("gpsimd", lambda e, i=i: e.dma_start(out=wg_sb[i][:], in_=wg_d[i * 128:(i + 1) * 128, :])))
            consts.append(p.add("gpsimd", lambda e, i=i: e.dma_start(out=wo_sb[i][:], in_=wo_d[i * 128:(i + 1) * 128, :])))
        consts.append(p.add("gpsimd", lambda e: e.dma_start(out=gnwr[:], in_=gnw_d[:])))
        consts.append(p.add("sync", lambda e: e.dma_start(out=tril[:], in_=tril_d[:])))
        consts.append(p.add("sync", lambda e: e.dma_start(out=ident[:], in_=identd[:])))
        for nm, sb_ in (("s1", s1_sb), ("s2", s2_sb), ("s3", s3_sb)):
            consts.append(p.add("sync", lambda e, nm=nm, sb_=sb_: e.dma_start(
                out=sb_[:], in_={"s1": s1d, "s2": s2d, "s3": s3d}[nm][:])))
        consts.append(p.add("sync", lambda e: e.dma_start(out=d1_sb[:], in_=d1d[:])))
        consts.append(p.add("sync", lambda e: e.dma_start(out=d2_sb[:], in_=d2d[:])))
        nz = p.add("vector", lambda e: (e.memset(ones1[:], 1.0),
                                        e.memset(qsc[:], QSCALE_LN),
                                        e.memset(scratch["eps"][:], 1e-5))[-1])
        consts.append(nz)
        nb1 = p.add("tensor", lambda e: e.matmul(ps_g[:, 0:512], ones1[:], gnwr[0:1, 0:512],
                                                 start=True, stop=True), consts)
        nb1e = p.add("vector", lambda e: e.tensor_copy(gnwb[:, 0:512], ps_g[:, 0:512]), [nb1])
        nb2 = p.add("tensor", lambda e: e.matmul(ps_g[:, 0:512], ones1[:], gnwr[0:1, 512:1024],
                                                 start=True, stop=True), [nb1e])
        nb2e = p.add("vector", lambda e: e.tensor_copy(gnwb[:, 512:1024], ps_g[:, 0:512]), [nb2])
        sin1 = p.add("vector", lambda e: [e.scalar_tensor_tensor(
            stmp[:, h * 256:(h + 1) * 256], s3_sb[:, h * 256:(h + 1) * 256],
            d2_sb[:, h:h + 1], s2_sb[:, h * 256:(h + 1) * 256], ALU.mult, ALU.add)
            for h in range(4)][-1], consts)
        sin2 = p.add("vector", lambda e: [e.scalar_tensor_tensor(
            S_sb[:, h * 256:(h + 1) * 256], stmp[:, h * 256:(h + 1) * 256],
            d1_sb[:, h:h + 1], s1_sb[:, h * 256:(h + 1) * 256], ALU.mult, ALU.add)
            for h in range(4)][-1], [sin1])
        cready = consts + [nb2e, sin2]

        store_nodes = []
        last_ln = [None, None]
        prev_tr = None
        prev_su = [sin2] * 4
        prev_su_any = sin2
        prev_cpn = None
        prev_scast_reader = None
        prev_scast = sin2
        prev_vc = [[], []]
        prev_qT_reader = None
        prev_kT_reader = None
        prev_G_reader = None
        prev_Aev = [None, None]
        prev_Amm = [None, None]
        prev_oev = [None, None]
        prev_gated = None
        prev_x3 = [None, None]
        chg = 0
        for it in range(16):
            buf = it % 2
            xl = p.add("sync", lambda e, it=it, buf=buf: e.dma_start(
                out=xtile[buf][:], in_=x2in[it * 128:(it + 1) * 128, :]),
                [last_ln[buf]])
            ln_deps = [xl, prev_tr] + (cready if it == 0 else [])
            ln = emit_ln_tile(ctx, xtile[buf][:], xn_bf[buf], scratch, ln_deps)
            last_ln[buf] = ln
            tevs = emit_transposes(ctx, xn_bf[buf], xnT, ident, ps_tp, ln, prev_round=prev_tr)
            tdep = tevs[-1]

            def qmm(e):
                last = None
                for h in range(4):
                    for kc in range(8):
                        last = e.matmul(ps_q[:, h * 128:(h + 1) * 128],
                                        wq_sb[kc][:, h * 128:(h + 1) * 128],
                                        xnT[kc][:], start=(kc == 0), stop=(kc == 7))
                return last
            qn = p.add("tensor", qmm, [tdep] + ([prev_qT_reader] if prev_qT_reader else []))
            qev = p.add("vector", lambda e: e.tensor_copy(qT_t[:], ps_q[:]), [qn])

            gn0 = p.add("tensor", lambda e: [e.matmul(ps_g[:, 0:512], xnT[kc][:],
                        wg_sb[kc][:, 0:512], start=(kc == 0), stop=(kc == 7)) for kc in range(8)][-1],
                        [tdep, nb2e] + ([prev_gated] if prev_gated else []))
            ge0 = p.add("scalar", lambda e: e.activation(gsil[:, 0:512], ps_g[:, 0:512], AF.Silu), [gn0])
            gn1 = p.add("tensor", lambda e: [e.matmul(ps_g[:, 0:512], xnT[kc][:],
                        wg_sb[kc][:, 512:1024], start=(kc == 0), stop=(kc == 7)) for kc in range(8)][-1],
                        [ge0])
            ge1 = p.add("scalar", lambda e: e.activation(gsil[:, 512:1024], ps_g[:, 0:512], AF.Silu), [gn1])
            gsn = p.add("vector", lambda e: e.tensor_tensor(gs_sb[:], gsil[:], gnwb[:], ALU.mult),
                        [ge1, nb2e])

            kl = p.add("gpsimd", lambda e, it=it: e.dma_start(
                out=kT_t[:].rearrange("p (h t) -> p h t", h=4),
                in_=AP(kTd.tensor, it * 128, [[T, 128], [128 * T, 4], [1, 128]])),
                [prev_kT_reader])
            vtl = p.add("gpsimd", lambda e, it=it: e.dma_start(
                out=v_t[:], in_=vd[it * 128:(it + 1) * 128, :]),
                [prev_su_any])
            gl = p.add("sync", lambda e, it=it: e.dma_start(
                out=G_t[:].rearrange("p (h t) -> p h t", h=4),
                in_=AP(gd.tensor, it * 128, [[T, 128], [128 * T, 4], [1, 128]])),
                [prev_G_reader])

            e1 = p.add("scalar", lambda e: e.activation(expGs[:], G_t[:], AF.Exp,
                                                        scale=GATE_SCALE, bias=qsc[:, 0:1]),
                       [gl] + (cready if it == 0 else []))
            e2 = p.add("scalar", lambda e: e.activation(expnG[:], G_t[:], AF.Exp,
                                                        scale=-GATE_SCALE), [e1])
            g16 = p.add("vector", lambda e: e.tensor_scalar(
                gl16[:], AP(G_t, 63, [[512, 128], [128, 4], [64, 2]]),
                GATE_SCALE, None, ALU.mult), [gl, e2])
            dn2 = p.add("scalar", lambda e: e.activation(dcat[:], gl16[:], AF.Exp), [g16])
            def kgfn(e):
                last = None
                for h in range(4):
                    for ch in range(2):
                        c0 = h * 128 + ch * 64
                        last = e.activation(kgf[:, c0:c0 + 64], G_t[:, c0:c0 + 64], AF.Exp,
                                            scale=-GATE_SCALE, bias=gl16[:, 2 * h + ch:2 * h + ch + 1])
                return last
            kf = p.add("scalar", kgfn, [dn2])

            prods = p.add("vector", lambda e: (
                e.tensor_tensor(qg_sb[:], qT_t[:], expGs[:], ALU.mult),
                e.tensor_tensor(ki_sb[:], kT_t[:], expnG[:], ALU.mult),
                e.tensor_tensor(kg_sb[:], kT_t[:], kgf[:], ALU.mult))[-1],
                [qev, kl, kf] + ([prev_kT_reader] if prev_kT_reader else []))
            prev_qT_reader = prods
            prev_kT_reader = prods
            prev_G_reader = kf
            def kgtr(e):
                last = None
                for h in range(4):
                    last = e.transpose(ps_tp[:, h * 128:(h + 1) * 128],
                                       kg_sb[:, h * 128:(h + 1) * 128], ident[:])
                return last
            kt = p.add("tensor", kgtr, [prods, tdep])
            kte = p.add("vector", lambda e: e.tensor_copy(kgtok[:], ps_tp[:]), [kt])
            prev_tr = kte

            for ch in range(2):
                cb = chg % 2
                vl = p.add("gpsimd", lambda e, it=it, ch=ch, cb=cb: e.dma_start(
                    out=vc_sb[cb][:], in_=vd[it * 128 + ch * 64: it * 128 + ch * 64 + 64, :]),
                    prev_vc[cb])
                scast = p.add("vector", lambda e: e.tensor_copy(Sbf[:], S_sb[:]),
                              [n for n in prev_su if n] + ([prev_scast_reader] if prev_scast_reader else []))
                omm_last = None
                su_nodes = []
                for h in range(4):
                    ab_ = chg % 2
                    amm = p.add("tensor", lambda e, ch=ch, h=h, ab_=ab_: e.matmul(
                        ps_A[:, ab_ * 64:(ab_ + 1) * 64],
                        ki_sb[:, h * 128 + ch * 64: h * 128 + ch * 64 + 64],
                        qg_sb[:, h * 128 + ch * 64: h * 128 + ch * 64 + 64],
                        start=True, stop=True),
                        [prods] + ([prev_Aev[ab_]] if prev_Aev[ab_] else []))
                    aev = p.add("vector", lambda e, ab_=ab_: e.tensor_tensor(
                        A_sb[ab_][:], ps_A[:, ab_ * 64:(ab_ + 1) * 64], tril[:], ALU.mult),
                        [amm])
                    prev_Aev[ab_] = aev
                    um = p.add("tensor", lambda e, ch=ch, h=h: e.matmul(
                        ps_U[:], kgtok[ch * 64:ch * 64 + 64, h * 128:(h + 1) * 128],
                        v_t[ch * 64:ch * 64 + 64, h * 256:(h + 1) * 256],
                        start=True, stop=True),
                        [kte, vtl] + ([prev_su_any] if prev_su_any else []))
                    ob_ = chg % 2
                    omm = p.add("tensor", lambda e, ch=ch, h=h, cb=cb, ab_=ab_, ob_=ob_: (
                        e.matmul(ps_o[:, ob_ * 256:(ob_ + 1) * 256], A_sb[ab_][:],
                                 vc_sb[cb][:, h * 256:(h + 1) * 256], start=True, stop=False),
                        e.matmul(ps_o[:, ob_ * 256:(ob_ + 1) * 256],
                                 qg_sb[:, h * 128 + ch * 64: h * 128 + ch * 64 + 64],
                                 Sbf[:, h * 256:(h + 1) * 256], start=False, stop=True))[-1],
                        [aev, scast, vl] + ([prev_oev[ob_]] if prev_oev[ob_] else []))
                    su = p.add("vector", lambda e, ch=ch, h=h: e.scalar_tensor_tensor(
                        S_sb[:, h * 256:(h + 1) * 256], S_sb[:, h * 256:(h + 1) * 256],
                        dcat[:, 2 * h + ch:2 * h + ch + 1], ps_U[:],
                        ALU.mult, ALU.add), [um, scast])
                    prev_su[h] = su
                    prev_su_any = su
                    su_nodes.append(su)
                    oev = p.add("vector", lambda e, ch=ch, h=h, ob_=ob_: e.tensor_copy(
                        oraw[ch * 64:ch * 64 + 64, h * 256:(h + 1) * 256],
                        ps_o[:, ob_ * 256:(ob_ + 1) * 256]), [omm])
                    prev_oev[ob_] = oev
                    omm_last = omm
                prev_vc[cb] = [omm_last, um]
                prev_scast_reader = omm_last
                chg += 1

            sqn = p.add("scalar", lambda e: e.activation(osq[:], oraw[:], AF.Square),
                        [prev_oev[0], prev_oev[1]] + ([prev_cpn] if prev_cpn else []))
            msn = p.add("vector", lambda e: e.reduce_sum(
                ms_sb[:], osq[:].rearrange("p (h v) -> p h v", h=4), axis=AX.X), [sqn])
            lnn = p.add("scalar", lambda e: e.activation(ms2_sb[:], ms_sb[:], AF.Ln,
                                                         scale=1.0 / 256.0,
                                                         bias=scratch["eps"][:, 0:1]), [msn])
            cpn = p.add("vector", lambda e: e.tensor_copy(rstd4[:], ms2_sb[:]), [lnn])
            prev_cpn = cpn
            exn = p.add("scalar", lambda e: e.activation(rstd4[:], rstd4[:], AF.Exp,
                                                         scale=-0.5), [cpn])
            gtn = p.add("vector", lambda e: [e.scalar_tensor_tensor(
                gated[:, h * 256:(h + 1) * 256], oraw[:, h * 256:(h + 1) * 256],
                rstd4[:, h:h + 1], gs_sb[:, h * 256:(h + 1) * 256], ALU.mult, ALU.mult)
                for h in range(4)][-1], [exn, gsn])
            prev_gated = gtn

            tevs2 = emit_transposes(ctx, gated, xnT, ident, ps_tp, gtn, prev_round=prev_tr)
            tdep2 = tevs2[-1]
            prev_tr = tdep2
            for half in range(2):
                xn_ = p.add("tensor", lambda e, half=half: [e.matmul(
                    ps_x[:, 0:512], xnT[kc][:], wo_sb[kc][:, half * 512:(half + 1) * 512],
                    start=(kc == 0), stop=(kc == 7)) for kc in range(8)][-1],
                    [tdep2] + ([store_nodes[-1]] if store_nodes else []))
                xe = p.add("vector", lambda e, half=half, buf=buf: e.tensor_tensor(
                    x3_sb[buf][:, half * 512:(half + 1) * 512], ps_x[:, 0:512],
                    xtile[buf][:, half * 512:(half + 1) * 512], ALU.add),
                    [xn_] + ([prev_x3[buf]] if prev_x3[buf] and half == 0 else []))
                tdep2 = xe
            st = p.add("sync", lambda e, it=it, buf=buf: e.dma_start(
                out=x3out[it * 128:(it + 1) * 128, :], in_=x3_sb[buf][:]), [tdep2])
            prev_x3[buf] = st
            last_ln[buf] = p.add("vector", lambda e, buf=buf: e.tensor_copy(
                scratch["sq"][:, 0:8], xtile[buf][:, 0:8]), [tdep2]) if False else last_ln[buf]
            last_ln[buf] = tdep2
            store_nodes.append(st)

        return ctx.finish(store_nodes)



RMAGIC = 12582912.0


def build_l1f(debug=False):
    with Ctx("l1f") as ctx:
        nc, p = ctx.nc, ctx.prog
        x3in = ctx.din("x3in", [T, D])
        packed = ctx.din("packed", [256, 1024], I32)
        wscale = ctx.din("wscale", [1, 1])
        bias_d = ctx.din("bias", [1, D])
        identd = ctx.din("ident", [128, 128], BF16)
        x4out = ctx.dout("x4", [T, D])
        wtdump = ctx.dout("wtdump", [1024, D], BF16) if debug else None
        xqdump = ctx.dout("xqdump", [T, D], BF16) if debug else None

        pk_sb = [ctx.sb(f"pk{i}", [128, 1024], I32) for i in range(2)]
        wt_sb = [ctx.sb(f"wt{i}", [128, D], BF16) for i in range(8)]
        wtT = [ctx.sb(f"wtT{i}", [128, D], BF16) for i in range(8)]
        wti = ctx.sb("wti", [128, D], I32)
        wtf = ctx.sb("wtf", [128, D], BF16)
        ident = ctx.sb("ident_sb", [128, 128], BF16)
        ones1 = ctx.sb("ones1", [1, 128], BF16)
        wsr = ctx.sb("wsr", [1, 1], BF16)
        wsb = ctx.sb("wsb", [128, 1])
        biasr = ctx.sb("biasr", [1, D], BF16)
        bias_bc = ctx.sb("bias_bc", [128, D])
        xtile = [ctx.sb(f"xt{i}", [128, D]) for i in range(2)]
        xn_f = ctx.sb("xn_f", [128, D])
        amax = ctx.sb("amax", [128, 1])
        c1 = ctx.sb("c1", [128, 1])
        c1b = ctx.sb("c1b", [128, 1])
        rcol = ctx.sb("rcol", [128, 1])
        col2 = ctx.sb("col2", [128, 1])
        t1 = ctx.sb("t1", [128, D])
        t2 = ctx.sb("t2", [128, D])
        xq = ctx.sb("xq", [128, D], BF16)
        xqT_g = [ctx.sb(f"xqT{i}", [128, 128], BF16) for i in range(8)]
        bx = ctx.sb("bx", [128, D])
        x4_sb = [ctx.sb(f"x4sb{i}", [128, D]) for i in range(2)]
        scratch = {k: ctx.sb("ln_" + k, [128, 1]) for k in
                   ["sum", "msq", "mu", "var", "rstd", "nb", "m2", "eps"]}
        scratch["sq"] = ctx.sb("ln_sq", [128, D])

        ps_tp = ctx.ps("ps_tp", [128, 512], BF16)
        ps_x = ctx.ps("ps_x", [128, 512])
        ps_b = ctx.ps("ps_b", [128, 512])

        consts = []
        consts.append(p.add("sync", lambda e: e.dma_start(out=pk_sb[0][:], in_=packed[0:128, :])))
        consts.append(p.add("sync", lambda e: e.dma_start(out=pk_sb[1][:], in_=packed[128:256, :])))
        consts.append(p.add("sync", lambda e: e.dma_start(out=ident[:], in_=identd[:])))
        consts.append(p.add("gpsimd", lambda e: e.dma_start(out=wsr[:], in_=wscale[:])))
        consts.append(p.add("gpsimd", lambda e: e.dma_start(out=biasr[:], in_=bias_d[:])))
        nz = p.add("vector", lambda e: (e.memset(ones1[:], 1.0),
                                        e.memset(scratch["eps"][:], 1e-5))[-1])
        consts.append(nz)
        unp = []
        for s in range(4):
            for tt in range(2):
                oc = s * 2 + tt
                u1 = p.add("vector", lambda e, s=s, tt=tt: e.tensor_scalar(
                    wti[:], pk_sb[tt][:], 2 * s, 3,
                    ALU.logical_shift_right, ALU.bitwise_and), consts + (unp[-1:] if unp else []))
                u2 = p.add("vector", lambda e: e.tensor_copy(wtf[:], wti[:]), [u1])
                u3 = p.add("vector", lambda e, oc=oc: e.tensor_scalar(
                    wt_sb[oc][:], wtf[:], 1.0, None, ALU.subtract), [u2])
                unp.append(u3)
        if debug:
            for oc in range(8):
                unp.append(p.add("sync", lambda e, oc=oc: e.dma_start(
                    out=wtdump[oc * 128:(oc + 1) * 128, :], in_=wt_sb[oc][:]), [unp[oc]]))
        trn = []
        for oc in range(8):
            tevs = emit_transposes(ctx, wt_sb[oc], wtT, ident, ps_tp, unp[oc],
                                   col0=oc * 128, prev_round=trn[-1] if trn else None)
            trn.append(tevs[-1])
        nb1 = p.add("tensor", lambda e: e.matmul(ps_b[:, 0:1], ones1[:], wsr[0:1, :],
                                                 start=True, stop=True), consts)
        nb1e = p.add("vector", lambda e: e.tensor_scalar(wsb[:], ps_b[:, 0:1],
                                                         1.0 / 127.0, None, ALU.mult), [nb1])
        nb2 = p.add("tensor", lambda e: e.matmul(ps_b[:, 0:512], ones1[:], biasr[0:1, 0:512],
                                                 start=True, stop=True), [nb1e])
        nb2e = p.add("vector", lambda e: e.tensor_copy(bias_bc[:, 0:512], ps_b[:, 0:512]), [nb2])
        nb3 = p.add("tensor", lambda e: e.matmul(ps_b[:, 0:512], ones1[:], biasr[0:1, 512:1024],
                                                 start=True, stop=True), [nb2e])
        nb3e = p.add("vector", lambda e: e.tensor_copy(bias_bc[:, 512:1024], ps_b[:, 0:512]), [nb3])
        cready = consts + [nb3e] + trn

        store_nodes = []
        last_ln = [None, None]
        prev_xq_reader = None
        prev_hp2 = None
        prev_tr = trn[-1]
        prev_x4 = [None, None]
        for it in range(NT):
            buf = it % 2
            xl = p.add("sync", lambda e, it=it, buf=buf: e.dma_start(
                out=xtile[buf][:], in_=x3in[it * 128:(it + 1) * 128, :]),
                [last_ln[buf]])
            ln_deps = [xl, prev_tr] + (cready if it == 0 else [])
            ln = emit_ln_tile(ctx, xtile[buf][:], xn_f, scratch, ln_deps)
            am = p.add("vector", lambda e: e.reduce_max(amax[:], xn_f[:], axis=AX.X,
                                                        apply_absolute_value=True), [ln])
            hpA = p.add("scalar", lambda e: e.activation(c1[:], amax[:], AF.Copy),
                        [am] + ([prev_hp2] if prev_hp2 else []))
            cl = p.add("vector", lambda e: e.tensor_scalar(c1[:], c1[:], 1e-5, None,
                                                           ALU.max), [hpA])
            hp = p.add("scalar", lambda e: e.activation(c1b[:], c1[:], AF.Copy), [cl])
            rc = p.add("vector", lambda e: e.reciprocal(rcol[:], c1b[:]), [hp])
            hp2 = p.add("scalar", lambda e: e.activation(col2[:], c1b[:], AF.Copy,
                                                         scale=wsb[:, 0:1]),
                        [rc, nb1e] + ([prev_xq_reader] if prev_xq_reader else []))
            prev_hp2 = hp2
            q1 = p.add("vector", lambda e: e.tensor_scalar(t1[:], xn_f[:], rcol[:, 0:1],
                                                           127.0, ALU.mult, ALU.mult), [rc, hp2])
            q2 = p.add("vector", lambda e: e.tensor_scalar(t2[:], t1[:], RMAGIC, RMAGIC,
                                                           ALU.add, ALU.subtract), [q1])
            q3 = p.add("vector", lambda e: e.tensor_scalar(xq[:], t2[:], -128.0, 127.0,
                                                           ALU.max, ALU.min),
                       [q2] + ([prev_xq_reader] if prev_xq_reader else []))
            if debug:
                q3 = p.add("gpsimd", lambda e, it=it: e.dma_start(
                    out=xqdump[it * 128:(it + 1) * 128, :], in_=xq[:]), [q3])
            xq_evs = emit_transposes(ctx, xq, xqT_g, ident, ps_tp, q3, prev_round=prev_tr)
            tdep = xq_evs[-1]
            prev_tr = tdep
            bxn = p.add("vector", lambda e, buf=buf: e.tensor_tensor(
                bx[:], bias_bc[:], xtile[buf][:], ALU.add), [ln, nb3e] + store_nodes[-1:])
            xlast = None
            for half in range(2):
                mm = p.add("tensor", lambda e, half=half: [e.matmul(
                    ps_x[:, 0:512], xqT_g[kc][:], wtT[kc][:, half * 512:(half + 1) * 512],
                    start=(kc == 0), stop=(kc == 7)) for kc in range(8)][-1],
                    [tdep] + ([xlast] if xlast else []) + (trn if it == 0 else []))
                xe = p.add("vector", lambda e, half=half, buf=buf: e.scalar_tensor_tensor(
                    x4_sb[buf][:, half * 512:(half + 1) * 512], ps_x[:, 0:512],
                    col2[:, 0:1], bx[:, half * 512:(half + 1) * 512], ALU.mult, ALU.add),
                    [mm, bxn, hp2] + ([prev_x4[buf]] if prev_x4[buf] and half == 0 else []))
                xlast = xe
            prev_xq_reader = xlast
            st = p.add("sync", lambda e, it=it, buf=buf: e.dma_start(
                out=x4out[it * 128:(it + 1) * 128, :], in_=x4_sb[buf][:]), [xlast])
            prev_x4[buf] = st
            last_ln[buf] = bxn
            store_nodes.append(st)

        return ctx.finish(store_nodes)



_NC_CACHE = {}


def _get_nc(name, builder):
    if name not in _NC_CACHE:
        _NC_CACHE[name] = builder()
    return _NC_CACHE[name]


def kernel(L0_attn_input_seq, L0_attn_Wq, L0_attn_bq, L0_attn_Wk, L0_attn_bk,
           L0_attn_Wv, L0_attn_bv, L0_ffn_W_fc, L0_ffn_W_proj,
           L1_attn_Wq, L1_attn_Wk, L1_attn_Wv, L1_attn_Wg, L1_attn_Wgk1,
           L1_attn_Wgk2, L1_attn_bgk2, L1_attn_Wo, L1_attn_gnw,
           L1_ffn_packed_weight, L1_ffn_weight_scale, L1_ffn_bias):
    x = np.ascontiguousarray(np.asarray(L0_attn_input_seq, dtype=np.float32))
    cores = list(range(NCORES))
    ident = np.eye(128, dtype=BF)
    identf = np.eye(128, dtype=np.float32)

    nc = _get_nc("l0a", build_l0a)
    maps = host_l0a_inputs(x, L0_attn_Wq, L0_attn_bq, L0_attn_Wk, L0_attn_bk,
                           L0_attn_Wv, L0_attn_bv)
    res = run_bass_kernel_spmd(nc, maps, core_ids=cores)
    x1c = [np.asarray(res.results[c]["x1"]) for c in cores]

    nc = _get_nc("l0f", build_l0f)
    maps = host_l0f_inputs(x1c, L0_ffn_W_fc, L0_ffn_W_proj)
    res = run_bass_kernel_spmd(nc, maps, core_ids=cores)
    x2c = [np.asarray(res.results[c]["x2"]) for c in cores]

    nc = _get_nc("l1a1", build_l1a_p1)
    wk = np.ascontiguousarray(np.asarray(L1_attn_Wk, dtype=np.float32))
    wv = np.ascontiguousarray(np.asarray(L1_attn_Wv, dtype=np.float32))
    wgk1 = np.ascontiguousarray(np.asarray(L1_attn_Wgk1, dtype=np.float32))
    wgk2 = np.ascontiguousarray(np.asarray(L1_attn_Wgk2, dtype=np.float32))
    bgk2 = np.asarray(L1_attn_bgk2, dtype=np.float32).reshape(1, 512)
    maps = [dict(x2in=x2c[c], wk=wk, wv=wv, wgk1=wgk1, wgk2=wgk2, bgk2=bgk2,
                 ident=ident, identf=identf) for c in cores]
    res = run_bass_kernel_spmd(nc, maps, core_ids=cores)
    sloc = [np.asarray(res.results[c]["sloc"]) for c in cores]
    dtot = [np.asarray(res.results[c]["dtot"]) for c in cores]
    kTs = [np.asarray(res.results[c]["kTd"]) for c in cores]
    vs_ = [np.asarray(res.results[c]["vd"]) for c in cores]
    gs_ = [np.asarray(res.results[c]["gd"]) for c in cores]

    nc = _get_nc("l1a2", build_l1a_p2)
    tril = np.zeros((64, 64), np.float32)
    for j in range(64):
        tril[j, j:] = 1.0
    zS = np.zeros((128, 1024), np.float32)
    zD = np.zeros((128, 4), np.float32)
    wq = np.ascontiguousarray(np.asarray(L1_attn_Wq, dtype=np.float32))
    wg = np.ascontiguousarray(np.asarray(L1_attn_Wg, dtype=np.float32))
    wo = np.ascontiguousarray(np.asarray(L1_attn_Wo, dtype=np.float32))
    gnw4 = np.tile(np.asarray(L1_attn_gnw, dtype=np.float32), 4).reshape(1, 1024)
    maps = []
    for c in cores:
        r = c % 4
        maps.append(dict(
            x2in=x2c[c], kTd=kTs[c], vd=vs_[c], gd=gs_[c],
            s1=sloc[c - 1] if r >= 1 else zS,
            s2=sloc[c - 2] if r >= 2 else zS,
            s3=sloc[c - 3] if r >= 3 else zS,
            d1=dtot[c - 1] if r >= 1 else zD,
            d2=dtot[c - 2] if r >= 2 else zD,
            wq=wq, wg=wg, wo=wo, gnw=gnw4, trilT=tril, ident=ident))
    res = run_bass_kernel_spmd(nc, maps, core_ids=cores)
    x3c = [np.asarray(res.results[c]["x3"]) for c in cores]

    nc = _get_nc("l1f", build_l1f)
    packed = np.ascontiguousarray(
        np.asarray(L1_ffn_packed_weight, dtype=np.int32).reshape(256, 1024))
    wsc = np.asarray(L1_ffn_weight_scale, dtype=np.float32).reshape(1, 1)
    bias = np.asarray(L1_ffn_bias, dtype=np.float32).reshape(1, D)
    maps = [dict(x3in=x3c[c], packed=packed, wscale=wsc, bias=bias, ident=ident)
            for c in cores]
    res = run_bass_kernel_spmd(nc, maps, core_ids=cores)

    out = np.zeros((2, 8192, D), np.float32)
    for c in cores:
        out[c // 4, (c % 4) * T:(c % 4 + 1) * T] = np.asarray(res.results[c]["x4"])
    return out
